# version 01
# baseline (speedup 1.0000x reference)
import numpy as np
from contextlib import ExitStack

import concourse.bass as bass
import concourse.mybir as mybir
from concourse.bass_utils import run_bass_kernel_spmd

F32 = mybir.dt.float32
BF16 = mybir.dt.bfloat16
AF = mybir.ActivationFunctionType
ALU = mybir.AluOpType
AX = mybir.AxisListType.X

NCORES = 8
TOK = 4096
NT = TOK // 128
D = 1024
DC = D // 128
SEQ = 2048
NBATCH = 2
NT_B = SEQ // 128



class P:

    def __init__(self, nc, stack):
        self.nc = nc
        self.stack = stack
        self.ops = {k: [] for k in "tvsyg"}
        self.sems = {}
        self.cnt = {}
        self.waited = {k: {} for k in "tvsyg"}

    _uid = [0]

    def name(self, base):
        P._uid[0] += 1
        return f"{base}_{P._uid[0]}"

    def sem(self, name):
        self.sems[name] = self.stack.enter_context(self.nc.semaphore(name))
        self.cnt[name] = 0
        return name

    def std_sems(self):
        for n in ["y_ld", "y_st", "g_ld", "g_st", "t_d", "v_d", "s_d", "bar"]:
            self.sem(n)

    def inc(self, name, k=1):
        self.cnt[name] += k
        return self.cnt[name]

    def cur(self, name):
        return self.cnt[name]

    AUTO_INC = {"v": "v_d", "s": "s_d"}

    def op(self, ek, fn, waits=(), inc=None, inck=1):
        ws = []
        for nm, v in waits:
            if v is None or v <= 0:
                continue
            prev = self.waited[ek].get(nm, 0)
            if v > prev:
                ws.append((self.sems[nm], v))
                self.waited[ek][nm] = v
        if inc is None and ek in self.AUTO_INC:
            inc = self.AUTO_INC[ek]
        if inc is False:
            inc = None
        tgt = None
        if inc is not None:
            tgt = (self.sems[inc], inck)
            self.inc(inc, inck)
        def run(eng, ws=ws, fn=fn, tgt=tgt):
            for h, v in ws:
                eng.wait_ge(h, v)
            ins = fn(eng)
            if tgt is not None and ins is not None:
                ins.then_inc(tgt[0], tgt[1])
        self.ops[ek].append(run)
        return self.cnt[inc] if inc else None


    def barrier(self):
        self.op("y", lambda e: None,
                waits=[("y_ld", self.cur("y_ld")), ("y_st", self.cur("y_st"))])
        self.op("g", lambda e: None,
                waits=[("g_ld", self.cur("g_ld")), ("g_st", self.cur("g_st"))])
        self.op("v", lambda e: None, waits=[("v_d", self.cur("v_d"))], inc=False)
        self.op("s", lambda e: None, waits=[("s_d", self.cur("s_d"))], inc=False)
        self.op("t", lambda e: None, waits=[("t_d", self.cur("t_d"))], inc=False)
        h = self.sems["bar"]
        for ek in "tvsyg":
            self.ops[ek].append(lambda eng, h=h: eng.sem_inc(h, 1))
        self.cnt["bar"] += 5
        tgt = self.cnt["bar"]
        for ek in "tvsyg":
            self.op(ek, lambda e: None, waits=[("bar", tgt)], inc=False)

    def finalize(self):
        self.barrier()

    def build_block(self):
        nc = self.nc
        with nc.Block() as block:
            @block.tensor
            def _(eng):
                for f in self.ops["t"]:
                    f(eng)
            @block.vector
            def _(eng):
                for f in self.ops["v"]:
                    f(eng)
            @block.scalar
            def _(eng):
                for f in self.ops["s"]:
                    f(eng)
            @block.sync
            def _(eng):
                for f in self.ops["y"]:
                    f(eng)
            @block.gpsimd
            def _(eng):
                for f in self.ops["g"]:
                    f(eng)



def host_consts():
    ident = np.eye(128, dtype=np.float32)
    u_incl = np.triu(np.ones((128, 128), dtype=np.float32), 0)
    u_excl = np.triu(np.ones((128, 128), dtype=np.float32), 1)
    masks = np.zeros((4, 128, 512), dtype=np.float32)
    for v in range(4):
        for j in range(4):
            blk = masks[v, :, j * 128:(j + 1) * 128]
            if j == v:
                blk[:] = np.triu(np.full((128, 128), -1e30, np.float32), 1)
            elif j > v:
                blk[:] = -1e30
    ones_col = np.ones((128, 1), np.float32)
    ones_row = np.ones((1, 128), np.float32)
    return {
        "c_ident": ident,
        "c_uincl": u_incl,
        "c_uexcl": u_excl,
        "c_masks": masks.reshape(4 * 128, 512),
        "c_ones_col": ones_col,
        "c_ones_row": ones_row,
    }


class Consts:

    def __init__(self, p, stack):
        nc = p.nc
        self.idt = stack.enter_context(nc.sbuf_tensor("c_idt", [128, 128], BF16))
        self.idt32 = stack.enter_context(nc.sbuf_tensor("c_idt32", [128, 128], F32))
        self.uincl = stack.enter_context(nc.sbuf_tensor("c_uincl_s", [128, 128], BF16))
        self.uincl32 = stack.enter_context(nc.sbuf_tensor("c_uincl32_s", [128, 128], F32))
        self.uexcl32 = stack.enter_context(nc.sbuf_tensor("c_uexcl32_s", [128, 128], F32))
        self.masks = stack.enter_context(nc.sbuf_tensor("c_masks_s", [128, 4, 512], F32))
        self.ones_col = stack.enter_context(nc.sbuf_tensor("c_onesc", [128, 1], F32))
        self.ones_row = stack.enter_context(nc.sbuf_tensor("c_onesr", [1, 128], F32))
        self._p = p
        self._stack = stack
        self._cols = {}
        d = p.nc_inputs
        p.op("g", lambda e: e.dma_start(out=self.idt[:], in_=d["c_ident"][:, :]), inc="g_ld", inck=16)
        p.op("y", lambda e: e.dma_start(out=self.idt32[:], in_=d["c_ident"][:, :]), inc="y_ld", inck=16)
        p.op("g", lambda e: e.dma_start(out=self.uincl[:], in_=d["c_uincl"][:, :]), inc="g_ld", inck=16)
        p.op("y", lambda e: e.dma_start(out=self.uincl32[:], in_=d["c_uincl"][:, :]), inc="y_ld", inck=16)
        p.op("y", lambda e: e.dma_start(out=self.uexcl32[:], in_=d["c_uexcl"][:, :]), inc="y_ld", inck=16)
        p.op("y", lambda e: e.dma_start(
            out=self.masks[:, :, :],
            in_=d["c_masks"].rearrange("(a p) b -> p a b", p=128)), inc="y_ld", inck=16)
        p.op("y", lambda e: e.dma_start(out=self.ones_col[:], in_=d["c_ones_col"][:, :]), inc="y_ld", inck=16)
        p.op("y", lambda e: e.dma_start(out=self.ones_row[:], in_=d["c_ones_row"][:, :]), inc="y_ld", inck=16)

        for v in [1e-5, float(np.log(128.0 ** -0.5))]:
            self.col(v)
        self.g_ready = p.cur("g_ld")
        self.y_ready = p.cur("y_ld")

    def col(self, value, name=None):
        if value in self._cols:
            return self._cols[value]
        nm = name or f"c_col{len(self._cols)}"
        t = self._stack.enter_context(self._p.nc.sbuf_tensor(nm, [128, 1], F32))
        self._p.op("v", lambda e, t=t, value=value: e.memset(t[:], value), inc="v_d")
        self._cols[value] = t[:, 0:1]
        return self._cols[value]



def emit_prep(p, st, stream_in, acc_init, XF, consts,
              gate=None, n_tiles=NT, tile0=0, ptp=None):
    nc = p.nc
    nbuf = 2 if n_tiles > 1 else 1
    xt = [st.enter_context(nc.sbuf_tensor(p.name(f"pr_xt{i}"), [128, D], F32)) for i in range(nbuf)]
    xt = [xt[i % nbuf] for i in range(2)]
    bns = st.enter_context(nc.sbuf_tensor(p.name("pr_bns"), [128, 12], F32))
    mv = st.enter_context(nc.sbuf_tensor(p.name("pr_mv"), [128, 2], F32))
    sst = st.enter_context(nc.sbuf_tensor(p.name("pr_sst"), [128, 3], F32))
    xb = [st.enter_context(nc.sbuf_tensor(p.name(f"pr_xb{i}"), [128, D], BF16)) for i in range(nbuf)]
    xb = [xb[i % nbuf] for i in range(2)]
    if ptp is None:
        ptp = [st.enter_context(nc.psum_tensor(p.name(f"pr_pt{i}"), [128, 512], BF16)) for i in range(2)]
    pg = st.enter_context(nc.psum_tensor(p.name("pr_pg"), [128, 8], F32)) if gate else None

    lnv = sst[:, 0:1]; rstd = sst[:, 1:2]; mean_s = sst[:, 2:3]
    epsc = consts.col(1e-5)
    eps_ready = p.cur("v_d")

    x_v = []
    xb_free = [0, 0]
    pt_free = [0, 0]
    gate_v = 0
    mv_free = 0
    sst_free = 0
    for i in range(n_tiles):
        t = tile0 + i
        b = i % 2
        p.op("y", lambda e, b=b, t=t: e.dma_start(
            out=xt[b][:], in_=stream_in[t * 128:(t + 1) * 128, :]),
            waits=[("v_d", x_v[i - 2] if i >= 2 else 0)],
            inc="y_ld", inck=16)
        if acc_init is not None:
            p.op("y", lambda e, b=b, t=t: e.dma_start(
                out=acc_init[t * 128:(t + 1) * 128, :], in_=xt[b][:]),
                waits=[("y_ld", p.cur("y_ld"))], inc="y_st", inck=16)
        ldv = p.cur("y_ld")
        p.op("v", lambda e, b=b: e.bn_stats(bns[:, 0:6], xt[b][:, 0:512]),
             waits=[("y_ld", ldv), ("s_d", mv_free)])
        bc = p.op("v", lambda e, b=b: e.bn_stats(bns[:, 6:12], xt[b][:, 512:1024]))
        vdone = p.op("v", lambda e: e.bn_aggr(mv[:, 0:2], bns[:, 0:12]),
                     waits=[("v_d", bc)])
        sc = p.op("s", lambda e: e.activation(out=lnv, in_=mv[:, 1:2], func=AF.Ln,
                                              bias=epsc),
                  waits=[("v_d", vdone), ("v_d", eps_ready), ("v_d", sst_free)])
        p.op("s", lambda e: e.activation(out=rstd, in_=lnv, func=AF.Exp, scale=-0.5),
             waits=[("s_d", sc)])
        mv_free = p.op("s", lambda e: e.activation(out=mean_s, in_=mv[:, 0:1],
                                                   func=AF.Identity))
        vdone = p.op("v", lambda e, b=b: e.tensor_scalar(
            out=xb[b][:], in0=xt[b][:], scalar1=mean_s, scalar2=rstd,
            op0=ALU.subtract, op1=ALU.mult),
            waits=[("t_d", xb_free[b]), ("s_d", mv_free)], inc="v_d")
        sst_free = vdone
        x_v.append(vdone)
        tdone = None
        for gidx in range(2):
            pb = ptp[gidx]
            for j in range(4):
                jj = gidx * 4 + j
                tdone = p.op("t", lambda e, b=b, jj=jj, pb=pb, j=j: e.transpose(
                    pb[:, j * 128:(j + 1) * 128], xb[b][:, jj * 128:(jj + 1) * 128],
                    consts.idt[:]),
                    waits=[("v_d", vdone), ("v_d", pt_free[gidx]),
                           ("g_ld", consts.g_ready)],
                    inc="t_d" if j == 3 else None)
            ev = p.op("v", lambda e, pb=pb, gidx=gidx, i=i: e.tensor_copy(
                out=XF[:, gidx * 4:(gidx + 1) * 4, i * 128:(i + 1) * 128],
                in_=pb.ap().rearrange("p (a b) -> p a b", a=4)),
                waits=[("t_d", tdone)], inc="v_d")
            pt_free[gidx] = ev
        xb_free[b] = tdone
        xf_ready = pt_free[1]
        if gate is not None:
            gdone = None
            for dc in range(DC):
                gdone = p.op("t", lambda e, dc=dc, i=i: e.matmul(
                    pg[:], XF[:, dc, i * 128:(i + 1) * 128],
                    gate["gk_sb"][:, dc, :], start=(dc == 0), stop=(dc == 7)),
                    waits=[("v_d", xf_ready), ("v_d", gate_v),
                           ("g_ld", gate.get("ready", 0))],
                    inc="t_d" if dc == 7 else None)
            gate_v = p.op("v", lambda e, i=i: e.tensor_copy(
                out=gate["logits_sb"][:, i, :], in_=pg[:]),
                waits=[("t_d", gdone)], inc="v_d")
    return {}



def emit_gate_l0(p, st, logits_sb, W8, n_tiles=NT, coeff=0.5, dbg=None, l2cap=None):
    nc = p.nc
    g2 = st.enter_context(nc.sbuf_tensor(p.name("g2_scr"), [128, NT, 8], F32))
    gis1 = st.enter_context(nc.sbuf_tensor(p.name("g_is1"), [128, NT, 8], F32))
    gm1 = st.enter_context(nc.sbuf_tensor(p.name("g_m1"), [128, NT], F32))
    gm2 = st.enter_context(nc.sbuf_tensor(p.name("g_m2"), [128, NT], F32))
    gm1s = st.enter_context(nc.sbuf_tensor(p.name("g_m1s"), [128, NT], F32))
    gm2s = st.enter_context(nc.sbuf_tensor(p.name("g_m2s"), [128, NT], F32))
    gd = st.enter_context(nc.sbuf_tensor(p.name("g_d"), [128, NT], F32))
    ge2 = st.enter_context(nc.sbuf_tensor(p.name("g_e2"), [128, NT], F32))
    ge1 = st.enter_context(nc.sbuf_tensor(p.name("g_e1"), [128, NT], F32))
    if dbg is not None:
        dbg.update(dict(g2=g2, gis1=gis1, gm1=gm1, gm2=gm2, gm1s=gm1s,
                        gm2s=gm2s, gd=gd, ge2=ge2, ge1=ge1))
    c = None
    for i in range(n_tiles):
        c = p.op("v", lambda e, i=i: e.reduce_max(gm1[:, i:i + 1], logits_sb[:, i, :],
                                                  axis=AX),
                 inc="v_d" if i == n_tiles - 1 else None)
    sd = p.op("s", lambda e: e.activation(out=gm1s[:, 0:n_tiles],
                                          in_=gm1[:, 0:n_tiles], func=AF.Identity),
              waits=[("v_d", c)], inc="s_d")
    for i in range(n_tiles):
        gl = logits_sb[:, i, :]
        is1 = gis1[:, i, :]; l2 = g2[:, i, :]
        cc = p.op("v", lambda e, gl=gl, is1=is1, i=i: e.tensor_scalar(
            out=is1, in0=gl, scalar1=gm1s[:, i:i + 1], scalar2=None, op0=ALU.is_ge),
            waits=[("s_d", sd)])
        cc = p.op("v", lambda e, gl=gl, is1=is1, l2=l2: e.scalar_tensor_tensor(
            out=l2, in0=is1, scalar=-1e30, in1=gl, op0=ALU.mult, op1=ALU.add),
            waits=[("v_d", cc)])
        if l2cap is not None:
            p.op("v", lambda e, l2=l2, i=i: e.tensor_copy(out=l2cap[:, i, :], in_=l2),
                 waits=[("v_d", cc)])
        p.op("v", lambda e, l2=l2, i=i: e.reduce_max(gm2[:, i:i + 1], l2, axis=AX),
             waits=[("v_d", cc)])
    c = p.op("v", lambda e: e.tensor_sub(out=gd[:, 0:n_tiles], in0=gm2[:, 0:n_tiles],
                                         in1=gm1[:, 0:n_tiles]),
             waits=[("v_d", p.cur("v_d"))])
    p.op("s", lambda e: e.activation(out=gm2s[:, 0:n_tiles], in_=gm2[:, 0:n_tiles],
                                     func=AF.Identity), waits=[("v_d", c)])
    p.op("s", lambda e: e.activation(out=ge2[:, 0:n_tiles], in_=gd[:, 0:n_tiles],
                                     func=AF.Sigmoid))
    sd = p.op("s", lambda e: e.activation(out=ge1[:, 0:n_tiles], in_=gd[:, 0:n_tiles],
                                          func=AF.Sigmoid, scale=-1.0), inc="s_d")
    for i in range(n_tiles):
        is1 = gis1[:, i, :]; l2 = g2[:, i, :]; w8 = W8[:, i, :]
        cc = p.op("v", lambda e, l2=l2, i=i: e.tensor_scalar(
            out=l2, in0=l2, scalar1=gm2s[:, i:i + 1], scalar2=None, op0=ALU.is_ge),
            waits=[("s_d", sd), ("v_d", p.cur("v_d"))])
        cc = p.op("v", lambda e, is1=is1, w8=w8, i=i: e.tensor_scalar(
            out=w8, in0=is1, scalar1=ge1[:, i:i + 1], scalar2=coeff,
            op0=ALU.mult, op1=ALU.mult), waits=[("v_d", cc)])
        cc = p.op("v", lambda e, l2=l2, i=i: e.tensor_scalar(
            out=l2, in0=l2, scalar1=ge2[:, i:i + 1], scalar2=coeff,
            op0=ALU.mult, op1=ALU.mult), waits=[("v_d", cc)])
        p.op("v", lambda e, w8=w8, l2=l2: e.tensor_add(out=w8, in0=w8, in1=l2),
             waits=[("v_d", cc)])
    return p.cur("v_d")



def emit_ffn_passB(p, st, XF, W8, acc, wspec, consts):
    nc = p.nc
    NE = wspec["n_experts"]; HH = wspec["h_half"]
    ph0 = [st.enter_context(nc.psum_tensor(p.name(f"fb_h0{i}"), [128, 512], F32)) for i in range(2)]
    ph1 = [st.enter_context(nc.psum_tensor(p.name(f"fb_h1{i}"), [128, 512], F32)) for i in range(2)]
    pout = [st.enter_context(nc.psum_tensor(p.name(f"fb_o{i}"), [128, 512], F32)) for i in range(4)]
    stmp = [st.enter_context(nc.sbuf_tensor(p.name(f"fb_st{i}"), [128, 512], F32)) for i in range(2)]
    mfm = st.enter_context(nc.sbuf_tensor(p.name("fb_mfm"), [128, HH, 512], BF16))
    tmp = [st.enter_context(nc.sbuf_tensor(p.name(f"fb_tmp{i}"), [128, 512], F32)) for i in range(4)]

    iters = [(n, h) for n in range(NE) for h in range(2)]
    wld = {}
    def load_iter(i, which, wait_td):
        n, h = iters[i]
        for dst, src in wspec["load"](n, h, i % 2, which):
            p.op("g", lambda e, dst=dst, src=src: e.dma_start(out=dst, in_=src),
                 waits=[("t_d", wait_td)], inc="g_ld", inck=16)
        wld[(which, i)] = p.cur("g_ld")

    load_iter(0, "awg", 0)
    load_iter(0, "wo", 0)
    h_end = {}
    p2_end = {}
    mvac = {}
    ovac = {}
    gvac = {}
    ring = [0]
    for i, (n, h) in enumerate(iters):
        if i + 1 < len(iters):
            load_iter(i + 1, "awg", h_end.get(i - 1, 0))
        wab = wspec["wa_sb"][i % 2]
        wgb = wspec["wg_sb"][i % 2]
        for tg in range(8):
            for hc in range(HH):
                par = hc % 2
                tdone = None
                for dc in range(DC):
                    p.op("t", lambda e, dc=dc, hc=hc, tg=tg, par=par, wab=wab: e.matmul(
                        ph0[par][:], wab[:, dc, hc * 128:(hc + 1) * 128],
                        XF[:, dc, tg * 512:(tg + 1) * 512],
                        start=(dc == 0), stop=(dc == 7)),
                        waits=[("g_ld", wld[("awg", i)]),
                               ("v_d", mvac.get(par, 0))])
                for dc in range(DC):
                    tdone = p.op("t", lambda e, dc=dc, hc=hc, tg=tg, par=par, wgb=wgb: e.matmul(
                        ph1[par][:], wgb[:, dc, hc * 128:(hc + 1) * 128],
                        XF[:, dc, tg * 512:(tg + 1) * 512],
                        start=(dc == 0), stop=(dc == 7)),
                        inc="t_d" if dc == 7 else None)
                sd = p.op("s", lambda e, par=par: e.activation(
                    out=stmp[par][:], in_=ph0[par][:], func=AF.Silu),
                    waits=[("t_d", tdone), ("v_d", mvac.get(par, 0))], inc="s_d")
                mvac[par] = p.op("v", lambda e, par=par, hc=hc: e.tensor_tensor(
                    out=mfm[:, hc, :], in0=stmp[par][:], in1=ph1[par][:], op=ALU.mult),
                    waits=[("s_d", sd)], inc="v_d")
            h_end[i] = p.cur("t_d")
            m_ready = mvac[(HH - 1) % 2]
            for eh in range(2):
                for tt in range(4):
                    tdone = None
                    for hc in range(HH):
                        tdone = p.op("t", lambda e, hc=hc, tt=tt, eh=eh: e.matmul(
                            pout[tt][:],
                            mfm[:, hc, tt * 128:(tt + 1) * 128],
                            wspec["wo_sb"][:, hc, eh * 512:(eh + 1) * 512],
                            start=(hc == 0), stop=(hc == HH - 1)),
                            waits=[("g_ld", wld[("wo", i)]),
                                   ("v_d", m_ready),
                                   ("v_d", ovac.get(tt, 0))],
                            inc="t_d" if hc == HH - 1 else None)
                    bix = ring[0] % 4
                    ring[0] += 1
                    tile = tg * 4 + tt
                    ovac[tt] = p.op("v", lambda e, tt=tt, bix=bix, tile=tile, n=n: e.tensor_scalar(
                        out=tmp[bix][:], in0=pout[tt][:], scalar1=W8[:, tile, n:n + 1],
                        scalar2=None, op0=ALU.mult),
                        waits=[("t_d", tdone), ("g_st", gvac.get(bix, 0))],
                        inc="v_d")
                    gvac[bix] = p.op("g", lambda e, bix=bix, tile=tile, eh=eh: e.dma_start(
                        out=acc[tile * 128:(tile + 1) * 128, eh * 512:(eh + 1) * 512],
                        in_=tmp[bix][:], accum_op=ALU.add),
                        waits=[("v_d", ovac[tt])], inc="g_st", inck=16)
        p2_end[i] = p.cur("t_d")
        if i + 1 < len(iters):
            load_iter(i + 1, "wo", p2_end[i])
    return


def ffn_l0_wspec(p, st, d):
    nc = p.nc
    HH = 8
    wa = [st.enter_context(nc.sbuf_tensor(p.name(f"w0s{i}"), [128, DC, HH * 128], BF16)) for i in range(2)]
    wg = [st.enter_context(nc.sbuf_tensor(p.name(f"w1s{i}"), [128, DC, HH * 128], BF16)) for i in range(2)]
    wo = st.enter_context(nc.sbuf_tensor(p.name("wos"), [128, HH, D], BF16))
    w0_d, w1_d, wo_d = d["L0_ffn_w0_kernel"], d["L0_ffn_w1_kernel"], d["L0_ffn_wo_kernel"]

    def load(n, h, slot, which):
        out = []
        if which == "awg":
            src0 = w0_d[n, :, h * 1024:(h + 1) * 1024].rearrange("(dc p) hh -> p dc hh", p=128)
            src1 = w1_d[n, :, h * 1024:(h + 1) * 1024].rearrange("(dc p) hh -> p dc hh", p=128)
            out.append((wa[slot][:, :, :], src0))
            out.append((wg[slot][:, :, :], src1))
        else:
            srco = wo_d[n, h * 1024:(h + 1) * 1024, :].rearrange("(hc p) e -> p hc e", p=128)
            out.append((wo[:, :, :], srco))
        return out

    return {"n_experts": 8, "h_half": HH, "wa_sb": wa, "wg_sb": wg,
            "wo_sb": wo, "load": load}


def ffn_l1_wspec(p, st, d):
    nc = p.nc
    HH = 6
    wa = [st.enter_context(nc.sbuf_tensor(p.name(f"cfa{i}"), [128, DC, HH * 128], BF16)) for i in range(2)]
    wg = [st.enter_context(nc.sbuf_tensor(p.name(f"cfg{i}"), [128, DC, HH * 128], BF16)) for i in range(2)]
    wo = st.enter_context(nc.sbuf_tensor(p.name("cps"), [128, HH, D], BF16))
    cfc, cpr = d["L1_ffn_c_fc"], d["L1_ffn_c_proj"]

    def load(n, h, slot, which):
        out = []
        if which == "awg":
            a0 = h * 768
            srca = cfc[n, :, a0:a0 + 768].rearrange("(dc p) hh -> p dc hh", p=128)
            srcg = cfc[n, :, 1536 + a0:1536 + a0 + 768].rearrange("(dc p) hh -> p dc hh", p=128)
            out.append((wa[slot][:, :, :], srca))
            out.append((wg[slot][:, :, :], srcg))
        else:
            srco = cpr[n, h * 768:(h + 1) * 768, :].rearrange("(hc p) e -> p hc e", p=128)
            out.append((wo[:, :, :], srco))
        return out

    return {"n_experts": 8, "h_half": HH, "wa_sb": wa, "wg_sb": wg,
            "wo_sb": wo, "load": load}



def emit_s_ffn_l0(p, d, consts, stream_in, stream_out):
    nc = p.nc
    with ExitStack() as st:
        XF = st.enter_context(nc.sbuf_tensor(p.name("ff_XF"), [128, DC, TOK], BF16))
        W8 = st.enter_context(nc.sbuf_tensor(p.name("ff_W8"), [128, NT, 8], F32))
        gk = st.enter_context(nc.sbuf_tensor(p.name("ff_gk"), [128, DC, 8], BF16))
        logits = st.enter_context(nc.sbuf_tensor(p.name("ff_gl"), [128, NT, 8], F32))
        p.op("g", lambda e: e.dma_start(
            out=gk[:, :, :],
            in_=d["L0_ffn_gate_kernel"][:, :].rearrange("(dc p) e -> p dc e", p=128)),
            inc="g_ld", inck=16)
        gk_ready = p.cur("g_ld")
        with ExitStack() as stp:
            emit_prep(p, stp, stream_in, stream_out, XF, consts,
                      gate={"gk_sb": gk, "logits_sb": logits, "ready": gk_ready})
            p.barrier()
        with ExitStack() as stg:
            emit_gate_l0(p, stg, logits, W8)
            p.barrier()
        with ExitStack() as stb:
            wspec = ffn_l0_wspec(p, stb, d)
            emit_ffn_passB(p, stb, XF, W8, stream_out, wspec, consts)
            p.barrier()




def emit_attn_l0(p, d, consts, stream_in, stream_out):
    nc = p.nc
    with ExitStack() as st:
        XFb = st.enter_context(nc.sbuf_tensor(p.name("at_XF"), [128, DC, SEQ], BF16))
        Vb = st.enter_context(nc.sbuf_tensor(p.name("at_V"), [128, NT_B, D], BF16))
        qk_sb = st.enter_context(nc.sbuf_tensor(p.name("at_qk"), [128, DC, D], BF16))
        ov_sb = st.enter_context(nc.sbuf_tensor(p.name("at_ov"), [128, DC, D], BF16))
        Qf = st.enter_context(nc.sbuf_tensor(p.name("at_Qf"), [128, DC, 128], BF16))
        scores = st.enter_context(nc.sbuf_tensor(p.name("at_sc"), [128, SEQ], F32))
        P32 = st.enter_context(nc.sbuf_tensor(p.name("at_p32"), [128, SEQ], F32))
        Pbf = st.enter_context(nc.sbuf_tensor(p.name("at_pbf"), [128, SEQ], BF16))
        PT = st.enter_context(nc.sbuf_tensor(p.name("at_PT"), [128, NT_B, 128], BF16))
        sfx = st.enter_context(nc.sbuf_tensor(p.name("at_sfx"), [128, 4], F32))
        xr = [st.enter_context(nc.sbuf_tensor(p.name(f"at_xr{i}"), [128, D], F32)) for i in range(2)]
        osb = [st.enter_context(nc.sbuf_tensor(p.name(f"at_ou{i}"), [128, D], F32)) for i in range(2)]
        ps_s = st.enter_context(nc.psum_tensor(p.name("at_ps"), [128, 512], F32))
        ps_q = st.enter_context(nc.psum_tensor(p.name("at_pq"), [128, 128], F32))
        ps_v = st.enter_context(nc.psum_tensor(p.name("at_pv"), [128, D], F32))
        ps_o = st.enter_context(nc.psum_tensor(p.name("at_po"), [128, D], F32))
        ps_pt = st.enter_context(nc.psum_tensor(p.name("at_ppt"), [128, 512], BF16))
        ps_pt2 = st.enter_context(nc.psum_tensor(p.name("at_ppt2"), [128, 512], BF16))
        M = sfx[:, 0:1]; Mn = sfx[:, 1:2]; S = sfx[:, 2:3]; iS = sfx[:, 3:4]
        p.op("g", lambda e: e.dma_start(
            out=qk_sb[:, :, :],
            in_=d["L0_attn_qk"][:, :].rearrange("(dc p) e -> p dc e", p=128)),
            inc="g_ld", inck=16)
        p.op("g", lambda e: e.dma_start(
            out=ov_sb[:, :, :],
            in_=d["L0_attn_ov"][:, :].rearrange("(dc p) e -> p dc e", p=128)),
            inc="g_ld", inck=16)
        w_ready = p.cur("g_ld")
        for b in range(NBATCH):
            with ExitStack() as stp:
                emit_prep(p, stp, stream_in, None, XFb, consts,
                          n_tiles=NT_B, tile0=b * NT_B, ptp=[ps_pt, ps_pt2])
                p.barrier()
            vvac = {}
            for t in range(NT_B):
                ps = [ps_v, ps_o][t % 2]
                tdone = None
                for eh in range(2):
                    for dc in range(DC):
                        tdone = p.op("t", lambda e, t=t, dc=dc, eh=eh, ps=ps: e.matmul(
                            ps[:, eh * 512:(eh + 1) * 512],
                            XFb[:, dc, t * 128:(t + 1) * 128],
                            ov_sb[:, dc, eh * 512:(eh + 1) * 512],
                            start=(dc == 0), stop=(dc == 7)),
                            waits=[("g_ld", w_ready), ("s_d", vvac.get(t % 2, 0))],
                            inc="t_d" if (eh == 1 and dc == 7) else None)
                vvac[t % 2] = p.op("s", lambda e, t=t, ps=ps: e.activation(
                    out=Vb[:, t, :], in_=ps[:], func=AF.Identity),
                    waits=[("t_d", tdone)])
            v_done_s = p.cur("s_d")
            mask_v = 0
            epil_v = 0
            st_guard = [0, 0]
            sc_read_v = 0
            p32_read_v = 0
            store_land = {}
            for qt in range(NT_B):
                qf_ev = 0
                for dq in range(DC):
                    tdone = None
                    for dc in range(DC):
                        tdone = p.op("t", lambda e, dq=dq, dc=dc, qt=qt: e.matmul(
                            ps_q[:], qk_sb[:, dc, dq * 128:(dq + 1) * 128],
                            XFb[:, dc, qt * 128:(qt + 1) * 128],
                            start=(dc == 0), stop=(dc == 7)),
                            waits=[("s_d", qf_ev)],
                            inc="t_d" if dc == 7 else None)
                    qf_ev = p.op("s", lambda e, dq=dq: e.activation(
                        out=Qf[:, dq, :], in_=ps_q[:], func=AF.Identity),
                        waits=[("t_d", tdone)])
                ng = qt // 4 + 1
                sc_ev = 0
                for g in range(ng):
                    tdone = None
                    for dc in range(DC):
                        tdone = p.op("t", lambda e, g=g, dc=dc: e.matmul(
                            ps_s[:], Qf[:, dc, :],
                            XFb[:, dc, g * 512:(g + 1) * 512],
                            start=(dc == 0), stop=(dc == 7)),
                            waits=[("s_d", qf_ev if g == 0 else sc_ev),
                                   ("s_d", sc_ev)],
                            inc="t_d" if dc == 7 else None)
                    sc_ev = p.op("s", lambda e, g=g: e.activation(
                        out=scores[:, g * 512:(g + 1) * 512], in_=ps_s[:],
                        func=AF.Identity, scale=1.0 / 32.0),
                        waits=[("t_d", tdone), ("v_d", max(mask_v, sc_read_v))])
                L = ng * 512
                mask_v = p.op("v", lambda e, qt=qt, L=L: e.tensor_add(
                    out=scores[:, L - 512:L], in0=scores[:, L - 512:L],
                    in1=consts.masks[:, qt % 4, :]),
                    waits=[("s_d", sc_ev), ("y_ld", consts.y_ready)])
                mv = p.op("v", lambda e, L=L: e.reduce_max(M, scores[:, 0:L], axis=AX),
                          waits=[("v_d", mask_v)])
                sc_read_v = mv
                mv = p.op("v", lambda e: e.tensor_scalar(
                    out=Mn, in0=M, scalar1=-1.0, scalar2=None, op0=ALU.mult),
                    waits=[("v_d", mv)])
                se = p.op("s", lambda e, L=L: e.activation(
                    out=P32[:, 0:L], in_=scores[:, 0:L], func=AF.Exp, bias=Mn),
                    waits=[("v_d", max(mv, p32_read_v))])
                sv = p.op("v", lambda e, L=L: e.reduce_sum(S, P32[:, 0:L], axis=AX),
                          waits=[("s_d", se)])
                se = p.op("s", lambda e: e.activation(out=iS, in_=S, func=AF.Ln),
                          waits=[("v_d", sv)])
                se = p.op("s", lambda e: e.activation(out=iS, in_=iS, func=AF.Exp,
                                                      scale=-1.0),
                          waits=[("s_d", se)])
                pb_v = p.op("v", lambda e, L=L: e.tensor_scalar(
                    out=Pbf[:, 0:L], in0=P32[:, 0:L], scalar1=iS, scalar2=None,
                    op0=ALU.mult),
                    waits=[("s_d", se), ("t_d", st_guard[1])])
                p32_read_v = pb_v
                nkc = qt + 1
                for kc0 in range(0, nkc, 4):
                    kn = min(4, nkc - kc0)
                    tdone = None
                    for j in range(kn):
                        kc = kc0 + j
                        tdone = p.op("t", lambda e, kc=kc, j=j: e.transpose(
                            ps_pt[:, j * 128:(j + 1) * 128],
                            Pbf[:, kc * 128:(kc + 1) * 128], consts.idt[:]),
                            waits=[("v_d", pb_v), ("v_d", st_guard[0])],
                            inc="t_d" if j == kn - 1 else None)
                    last_tr = tdone
                    st_guard[0] = p.op("v", lambda e, kc0=kc0, kn=kn: e.tensor_copy(
                        out=PT[:, kc0:kc0 + kn, :],
                        in_=ps_pt.ap().rearrange("p (a b) -> p a b", a=4)[:, 0:kn, :]),
                        waits=[("t_d", tdone)])
                tdone = None
                for kc in range(nkc):
                    for eh in range(2):
                        tdone = p.op("t", lambda e, kc=kc, eh=eh, qt=qt: e.matmul(
                            ps_o[:, eh * 512:(eh + 1) * 512], PT[:, kc, :],
                            Vb[:, kc, eh * 512:(eh + 1) * 512],
                            start=(kc == 0), stop=(kc == nkc - 1)),
                            waits=[("v_d", st_guard[0]), ("s_d", v_done_s),
                                   ("v_d", epil_v)],
                            inc="t_d" if (kc == nkc - 1 and eh == 1) else None)
                st_guard[1] = last_tr
                t_glob = b * NT_B + qt
                p.op("y", lambda e, t_glob=t_glob, qt=qt: e.dma_start(
                    out=xr[qt % 2][:],
                    in_=stream_in[t_glob * 128:(t_glob + 1) * 128, :]),
                    waits=[("v_d", epil_v)], inc="y_ld", inck=16)
                ldv = p.cur("y_ld")
                epil_v = p.op("v", lambda e, qt=qt: e.tensor_tensor(
                    out=osb[qt % 2][:], in0=xr[qt % 2][:], in1=ps_o[:], op=ALU.add),
                    waits=[("t_d", tdone), ("y_ld", ldv),
                           ("y_st", store_land.get(qt - 2, 0))])
                p.op("y", lambda e, t_glob=t_glob, qt=qt: e.dma_start(
                    out=stream_out[t_glob * 128:(t_glob + 1) * 128, :],
                    in_=osb[qt % 2][:]),
                    waits=[("v_d", epil_v)], inc="y_st", inck=16)
                store_land[qt] = p.cur("y_st")
            p.barrier()




def emit_ffn_stats(p, d, consts, stream_in, gk_ap, T8_out, W8raw_out, cnt_out):
    nc = p.nc
    with ExitStack() as st:
        XFt = st.enter_context(nc.sbuf_tensor(p.name("fs_XF"), [128, DC, 128 * NT], BF16))
        gk = st.enter_context(nc.sbuf_tensor(p.name("fs_gk"), [128, DC, 8], BF16))
        logits = st.enter_context(nc.sbuf_tensor(p.name("fs_gl"), [128, NT, 8], F32))
        p.op("g", lambda e: e.dma_start(
            out=gk[:, :, :], in_=gk_ap.rearrange("(dc p) e -> p dc e", p=128)),
            inc="g_ld", inck=16)
        gk_ready = p.cur("g_ld")
        with ExitStack() as stp:
            emit_prep(p, stp, stream_in, None, XFt, consts,
                      gate={"gk_sb": gk, "logits_sb": logits, "ready": gk_ready})
            p.barrier()
        ex8 = st.enter_context(nc.sbuf_tensor(p.name("fs_ex8"), [128, NT, 8], F32))
        T8 = st.enter_context(nc.sbuf_tensor(p.name("fs_T8"), [128, NT, 8], F32))
        W8r = st.enter_context(nc.sbuf_tensor(p.name("fs_W8r"), [128, NT, 8], F32))
        gm1 = st.enter_context(nc.sbuf_tensor(p.name("fs_m1"), [128, NT], F32))
        gm2 = st.enter_context(nc.sbuf_tensor(p.name("fs_m2"), [128, NT], F32))
        gs8 = st.enter_context(nc.sbuf_tensor(p.name("fs_s8"), [128, NT], F32))
        gm1s = st.enter_context(nc.sbuf_tensor(p.name("fs_m1s"), [128, NT], F32))
        gm1n = st.enter_context(nc.sbuf_tensor(p.name("fs_m1n"), [128, NT], F32))
        gm2s = st.enter_context(nc.sbuf_tensor(p.name("fs_m2s"), [128, NT], F32))
        giS = st.enter_context(nc.sbuf_tensor(p.name("fs_iS"), [128, NT], F32))
        cnt_sb = st.enter_context(nc.sbuf_tensor(p.name("fs_cnt"), [1, 8], F32))
        ps_c = st.enter_context(nc.psum_tensor(p.name("fs_pc"), [1, 8], F32))
        for i in range(NT):
            p.op("v", lambda e, i=i: e.reduce_max(gm1[:, i:i + 1], logits[:, i, :],
                                                  axis=AX))
        c = p.cur("v_d")
        sd = p.op("s", lambda e: e.activation(out=gm1s[:, :], in_=gm1[:, :],
                                              func=AF.Identity), waits=[("v_d", c)])
        sd = p.op("s", lambda e: e.activation(out=gm1n[:, :], in_=gm1[:, :],
                                              func=AF.Identity, scale=-1.0))
        for i in range(NT):
            gl = logits[:, i, :]
            is1 = T8[:, i, :]; l2 = ex8[:, i, :]
            cc = p.op("v", lambda e, gl=gl, is1=is1, i=i: e.tensor_scalar(
                out=is1, in0=gl, scalar1=gm1s[:, i:i + 1], scalar2=None,
                op0=ALU.is_ge), waits=[("s_d", sd)])
            cc = p.op("v", lambda e, gl=gl, is1=is1, l2=l2: e.scalar_tensor_tensor(
                out=l2, in0=is1, scalar=-1e30, in1=gl, op0=ALU.mult, op1=ALU.add),
                waits=[("v_d", cc)])
            p.op("v", lambda e, l2=l2, i=i: e.reduce_max(gm2[:, i:i + 1], l2, axis=AX),
                 waits=[("v_d", cc)])
        c = p.cur("v_d")
        sd = p.op("s", lambda e: e.activation(out=gm2s[:, :], in_=gm2[:, :],
                                              func=AF.Identity), waits=[("v_d", c)])
        for i in range(NT):
            gl = logits[:, i, :]
            is1 = T8[:, i, :]; l2 = ex8[:, i, :]
            cc = p.op("v", lambda e, l2=l2, i=i: e.tensor_scalar(
                out=l2, in0=l2, scalar1=gm2s[:, i:i + 1], scalar2=None,
                op0=ALU.is_ge), waits=[("s_d", sd)])
            cc = p.op("v", lambda e, is1=is1, l2=l2: e.tensor_add(
                out=is1, in0=is1, in1=l2), waits=[("v_d", cc)])
            cc2 = p.op("v", lambda e, gl=gl, l2=l2, i=i: e.tensor_scalar(
                out=l2, in0=gl, scalar1=gm1s[:, i:i + 1], scalar2=None,
                op0=ALU.subtract), waits=[("v_d", cc)])
        c = p.cur("v_d")
        sd = p.op("s", lambda e: e.activation(out=ex8.ap().rearrange("p a b -> p (a b)"),
                                              in_=ex8.ap().rearrange("p a b -> p (a b)"),
                                              func=AF.Exp), waits=[("v_d", c)])
        for i in range(NT):
            p.op("v", lambda e, i=i: e.reduce_sum(gs8[:, i:i + 1], ex8[:, i, :],
                                                  axis=AX), waits=[("s_d", sd)])
        c = p.cur("v_d")
        sd = p.op("s", lambda e: e.activation(out=giS[:, :], in_=gs8[:, :],
                                              func=AF.Ln), waits=[("v_d", c)])
        sd = p.op("s", lambda e: e.activation(out=giS[:, :], in_=giS[:, :],
                                              func=AF.Exp, scale=-1.0),
                  waits=[("s_d", sd)])
        for i in range(NT):
            cc = p.op("v", lambda e, i=i: e.tensor_scalar(
                out=W8r[:, i, :], in0=ex8[:, i, :], scalar1=giS[:, i:i + 1],
                scalar2=None, op0=ALU.mult), waits=[("s_d", sd)])
            p.op("v", lambda e, i=i: e.tensor_mul(
                out=W8r[:, i, :], in0=W8r[:, i, :], in1=T8[:, i, :]),
                waits=[("v_d", cc)])
        vfin = p.cur("v_d")
        tdone = None
        for i in range(NT):
            tdone = p.op("t", lambda e, i=i: e.matmul(
                ps_c[:], consts.ones_col[:, :], T8[:, i, :],
                start=(i == 0), stop=(i == NT - 1)),
                waits=[("v_d", vfin), ("y_ld", consts.y_ready)],
                inc="t_d" if i == NT - 1 else None)
        cv = p.op("v", lambda e: e.tensor_copy(out=cnt_sb[:, :], in_=ps_c[:]),
                  waits=[("t_d", tdone)])
        p.op("y", lambda e: e.dma_start(out=T8_out[:, :, :], in_=T8[:, :, :]),
             waits=[("v_d", vfin)], inc="y_st", inck=16)
        p.op("y", lambda e: e.dma_start(out=W8raw_out[:, :, :], in_=W8r[:, :, :]),
             inc="y_st", inck=16)
        p.op("y", lambda e: e.dma_start(out=cnt_out[:, :], in_=cnt_sb[:, :]),
             waits=[("v_d", cv)], inc="y_st", inck=16)
        p.barrier()


def emit_capacity_resolve(p, st, d, consts, T8_in, W8raw_in, counts_all_ap,
                          core_lt_ap, W8, coeff=0.5):
    nc = p.nc
    T8 = st.enter_context(nc.sbuf_tensor(p.name("cr_T8"), [128, NT, 8], F32))
    W8r = st.enter_context(nc.sbuf_tensor(p.name("cr_W8r"), [128, NT, 8], F32))
    call = st.enter_context(nc.sbuf_tensor(p.name("cr_call"), [8, 8], F32))
    clt = st.enter_context(nc.sbuf_tensor(p.name("cr_clt"), [8, 1], F32))
    brow = st.enter_context(nc.sbuf_tensor(p.name("cr_brow"), [1, 8], F32))
    ps_b = st.enter_context(nc.psum_tensor(p.name("cr_pb"), [1, 8], F32))
    ps_S = st.enter_context(nc.psum_tensor(p.name("cr_pS"), [128, 8], F32))
    ps_c = st.enter_context(nc.psum_tensor(p.name("cr_pc"), [1, 8], F32))
    p.op("y", lambda e: e.dma_start(out=T8[:, :, :], in_=T8_in[:, :, :]),
         inc="y_ld", inck=16)
    p.op("y", lambda e: e.dma_start(out=W8r[:, :, :], in_=W8raw_in[:, :, :]),
         inc="y_ld", inck=16)
    p.op("y", lambda e: e.dma_start(out=call[:, :], in_=counts_all_ap[:, :]),
         inc="y_ld", inck=16)
    p.op("y", lambda e: e.dma_start(out=clt[:, :], in_=core_lt_ap[:, :]),
         inc="y_ld", inck=16)
    ldv = p.cur("y_ld")
    tdone = p.op("t", lambda e: e.matmul(ps_b[:], clt[:, :], call[:, :],
                                         start=True, stop=True),
                 waits=[("y_ld", ldv)], inc="t_d")
    bv = p.op("v", lambda e: e.tensor_scalar(
        out=brow[:, :], in0=ps_b[:], scalar1=-4096.0, scalar2=None, op0=ALU.add),
        waits=[("t_d", tdone)])
    for i in range(NT):
        tdone = p.op("t", lambda e, i=i: e.matmul(
            ps_S[:], consts.ones_row[:, :], brow[:, :], start=True, stop=False),
            waits=[("v_d", bv), ("y_ld", consts.y_ready)])
        tdone = p.op("t", lambda e, i=i: e.matmul(
            ps_S[:], consts.uexcl32[:, :], T8[:, i, :], start=False, stop=True),
            inc="t_d")
        cc = p.op("v", lambda e, i=i: e.tensor_scalar(
            out=W8[:, i, :], in0=ps_S[:], scalar1=0.0, scalar2=None, op0=ALU.is_lt),
            waits=[("t_d", tdone)])
        cc = p.op("v", lambda e, i=i: e.tensor_mul(
            out=W8[:, i, :], in0=W8[:, i, :], in1=W8r[:, i, :]),
            waits=[("v_d", cc)])
        w8fin = p.op("v", lambda e, i=i: e.tensor_scalar(
            out=W8[:, i, :], in0=W8[:, i, :], scalar1=coeff, scalar2=None,
            op0=ALU.mult), waits=[("v_d", cc)])
        tdone = p.op("t", lambda e, i=i: e.matmul(
            ps_c[:], consts.ones_col[:, :], T8[:, i, :], start=True, stop=True),
            waits=[("v_d", w8fin)], inc="t_d")
        bv = p.op("v", lambda e: e.tensor_add(out=brow[:, :], in0=brow[:, :],
                                              in1=ps_c[:]),
                  waits=[("t_d", tdone)])
    return p.cur("v_d")


def emit_s_ffn_l1(p, d, consts, stream_in, stream_out, T8_in, W8raw_in,
                  counts_all_ap, core_lt_ap):
    nc = p.nc
    with ExitStack() as st:
        XF = st.enter_context(nc.sbuf_tensor(p.name("f1_XF"), [128, DC, TOK], BF16))
        W8 = st.enter_context(nc.sbuf_tensor(p.name("f1_W8"), [128, NT, 8], F32))
        with ExitStack() as stp:
            emit_prep(p, stp, stream_in, stream_out, XF, consts)
            p.barrier()
        with ExitStack() as stg:
            emit_capacity_resolve(p, stg, d, consts, T8_in, W8raw_in,
                                  counts_all_ap, core_lt_ap, W8, coeff=0.5)
            p.barrier()
        with ExitStack() as stb:
            wspec = ffn_l1_wspec(p, stb, d)
            emit_ffn_passB(p, stb, XF, W8, stream_out, wspec, consts)
            p.barrier()




def emit_lightnet(p, d, consts, stream_in, stream_out, dbg=None):
    nc = p.nc
    H = 8
    with ExitStack() as st:
        wq = st.enter_context(nc.sbuf_tensor(p.name("ln_wq"), [128, DC, D], BF16))
        wk = st.enter_context(nc.sbuf_tensor(p.name("ln_wk"), [128, DC, D], BF16))
        wv = st.enter_context(nc.sbuf_tensor(p.name("ln_wv"), [128, DC, D], BF16))
        wo = st.enter_context(nc.sbuf_tensor(p.name("ln_wo"), [128, DC, D], BF16))
        wg1 = st.enter_context(nc.sbuf_tensor(p.name("ln_wg1"), [128, DC, 128], BF16))
        wg2 = st.enter_context(nc.sbuf_tensor(p.name("ln_wg2"), [128, D], BF16))
        gnr = st.enter_context(nc.sbuf_tensor(p.name("ln_gnr"), [1, D], F32))
        gnB = st.enter_context(nc.sbuf_tensor(p.name("ln_gnB"), [128, D], F32))
        qs = st.enter_context(nc.sbuf_tensor(p.name("ln_qs"), [128, NT_B, D], BF16))
        gg = st.enter_context(nc.sbuf_tensor(p.name("ln_gg"), [128, NT_B, D], BF16))
        g1s = st.enter_context(nc.sbuf_tensor(p.name("ln_g1s"), [128, 128], BF16))
        S8 = st.enter_context(nc.sbuf_tensor(p.name("ln_S"), [128, H, 128], F32))
        S8b = st.enter_context(nc.sbuf_tensor(p.name("ln_Sb"), [128, H, 128], BF16))
        Zc = st.enter_context(nc.sbuf_tensor(p.name("ln_Z"), [128, H], F32))
        Zb = st.enter_context(nc.sbuf_tensor(p.name("ln_Zb"), [128, H], F32))
        Zbs = st.enter_context(nc.sbuf_tensor(p.name("ln_Zbs"), [128, H], F32))
        cs8 = st.enter_context(nc.sbuf_tensor(p.name("ln_cs8"), [128, H], F32))
        Mold = st.enter_context(nc.sbuf_tensor(p.name("ln_Mo"), [128, H], F32))
        Mnew = st.enter_context(nc.sbuf_tensor(p.name("ln_Mn"), [128, H], F32))
        Mc = st.enter_context(nc.sbuf_tensor(p.name("ln_Mc"), [128, H], F32))
        dM = st.enter_context(nc.sbuf_tensor(p.name("ln_dM"), [128, H], F32))
        rr = st.enter_context(nc.sbuf_tensor(p.name("ln_r"), [128, H], F32))
        negM = st.enter_context(nc.sbuf_tensor(p.name("ln_ngM"), [128, H], F32))
        kci = [st.enter_context(nc.sbuf_tensor(p.name(f"ln_kci{i}"), [128, D], BF16)) for i in range(2)]
        vsb = [st.enter_context(nc.sbuf_tensor(p.name(f"ln_vsb{i}"), [128, D], BF16)) for i in range(2)]
        kc_sb = st.enter_context(nc.sbuf_tensor(p.name("ln_kc"), [128, D], BF16))
        kt_tm = st.enter_context(nc.sbuf_tensor(p.name("ln_ktm"), [128, D], BF16))
        L_bf = st.enter_context(nc.sbuf_tensor(p.name("ln_Lbf"), [128, D], BF16))
        Lcm = st.enter_context(nc.sbuf_tensor(p.name("ln_Lcm"), [128, D], BF16))
        Zi = st.enter_context(nc.sbuf_tensor(p.name("ln_Zi"), [128, D], F32))
        inv = st.enter_context(nc.sbuf_tensor(p.name("ln_inv"), [128, D], F32))
        qq = st.enter_context(nc.sbuf_tensor(p.name("ln_qq"), [128, D], BF16))
        at_sb = Lcm
        ssum = st.enter_context(nc.sbuf_tensor(p.name("ln_ss"), [128, 2], F32))
        osq = Zi
        o1 = inv
        o3 = st.enter_context(nc.sbuf_tensor(p.name("ln_o3"), [128, D], BF16))
        ofm = st.enter_context(nc.sbuf_tensor(p.name("ln_ofm"), [128, DC, 128], BF16))
        ql1 = st.enter_context(nc.sbuf_tensor(p.name("ln_ql1"), [128, D], BF16))
        sg1 = st.enter_context(nc.sbuf_tensor(p.name("ln_sg1"), [128, D], BF16))
        xr0 = st.enter_context(nc.sbuf_tensor(p.name("ln_xr"), [128, D], F32))
        osb0 = st.enter_context(nc.sbuf_tensor(p.name("ln_osb"), [128, D], F32))
        xr = [xr0, xr0]
        osb = [osb0, osb0]
        if dbg is not None:
            dbg["o_scr"] = osb0
        ps_a = st.enter_context(nc.psum_tensor(p.name("ln_pa"), [128, D], F32))
        ps_b = st.enter_context(nc.psum_tensor(p.name("ln_pb"), [128, D], F32))
        ps_o = st.enter_context(nc.psum_tensor(p.name("ln_po"), [128, D], F32))
        ps_t = st.enter_context(nc.psum_tensor(p.name("ln_pt"), [128, 512], BF16))
        ps_t2 = st.enter_context(nc.psum_tensor(p.name("ln_pt2"), [128, 512], BF16))
        kd = nc.dram_tensor(p.name("ln_kd"), [NT_B, 128, D], BF16, kind="Internal").ap()
        vd = nc.dram_tensor(p.name("ln_vd"), [NT_B, 128, D], BF16, kind="Internal").ap()
        qld = nc.dram_tensor(p.name("ln_qld"), [NT_B, 128, D], BF16, kind="Internal").ap()
        gld = nc.dram_tensor(p.name("ln_gld"), [NT_B, 128, D], BF16, kind="Internal").ap()
        for dst, nm in [(wq, "L1_attn_WqT"), (wk, "L1_attn_WkT"), (wv, "L1_attn_WvT"),
                        (wo, "L1_attn_WoT")]:
            p.op("g", lambda e, dst=dst, nm=nm: e.dma_start(
                out=dst[:, :, :], in_=d[nm].rearrange("(dc p) i -> p dc i", p=128)),
                inc="g_ld", inck=16)
        p.op("g", lambda e: e.dma_start(
            out=wg1[:, :, :], in_=d["L1_attn_Wg1T"].rearrange("(dc p) i -> p dc i", p=128)),
            inc="g_ld", inck=16)
        p.op("g", lambda e: e.dma_start(out=wg2[:, :], in_=d["L1_attn_Wg2T"][:, :]),
             inc="g_ld", inck=16)
        p.op("y", lambda e: e.dma_start(
            out=gnr[:, :], in_=d["L1_attn_g_norm_weight"].rearrange("(a i) -> a i", a=1)),
            inc="y_ld", inck=16)
        w_ready = p.cur("g_ld")
        gn_ldv = p.cur("y_ld")
        tdone = None
        for eh in range(2):
            tdone = p.op("t", lambda e, eh=eh: e.matmul(
                ps_a[:, eh * 512:(eh + 1) * 512], consts.ones_row[:, :],
                gnr[:, eh * 512:(eh + 1) * 512], start=True, stop=True),
                waits=[("y_ld", gn_ldv), ("y_ld", consts.y_ready)],
                inc="t_d" if eh == 1 else None)
        gn_v = p.op("v", lambda e: e.tensor_copy(out=gnB[:, :], in_=ps_a[:]),
                    waits=[("t_d", tdone)])
        lnsc = consts.col(float(np.log(128.0 ** -0.5)))
        epsc = consts.col(1e-5)
        col_ready = p.cur("v_d")

        for b in range(NBATCH):
            pa_free_s = 0
            pb_free = 0
            stash_st = {}
            for c in range(NT_B):
                tg = b * NT_B + c
                XFt_scope = ExitStack()
                with XFt_scope as stp:
                    XFt = stp.enter_context(nc.sbuf_tensor(p.name("ln_XFt"), [128, DC, 128], BF16))
                    emit_prep(p, stp, stream_in, None, XFt, consts,
                              n_tiles=1, tile0=tg, ptp=[ps_t, ps_t2])
                    xf_v = p.cur("v_d")
                    tdone = None
                    for h in range(H):
                        for dc in range(DC):
                            tdone = p.op("t", lambda e, h=h, dc=dc, XFt=XFt: e.matmul(
                                ps_a[:, h * 128:(h + 1) * 128],
                                wq[:, dc, h * 128:(h + 1) * 128], XFt[:, dc, :],
                                start=(dc == 0), stop=(dc == 7)),
                                waits=[("g_ld", w_ready), ("v_d", xf_v),
                                       ("v_d", gn_v), ("s_d", pa_free_s)],
                                inc="t_d" if (h == H - 1 and dc == 7) else None)
                    qa = p.op("s", lambda e: e.activation(out=ql1[:, :], in_=ps_a[:],
                                                          func=AF.Identity),
                              waits=[("t_d", tdone), ("y_st", stash_st.get("q", 0))])
                    p.op("y", lambda e, c=c: e.dma_start(out=qld[c, :, :], in_=ql1[:, :]),
                         waits=[("s_d", qa)], inc="y_st", inck=16)
                    stash_st["q"] = p.cur("y_st")
                    pa_free_q = qa
                    tdone = None
                    for h in range(H):
                        for dc in range(DC):
                            tdone = p.op("t", lambda e, h=h, dc=dc, XFt=XFt: e.matmul(
                                ps_b[:, h * 128:(h + 1) * 128],
                                wk[:, dc, h * 128:(h + 1) * 128], XFt[:, dc, :],
                                start=(dc == 0), stop=(dc == 7)),
                                waits=[("s_d", pb_free)],
                                inc="t_d" if (h == H - 1 and dc == 7) else None)
                    ka = p.op("s", lambda e: e.activation(out=kc_sb[:, :], in_=ps_b[:],
                                                          func=AF.Identity),
                              waits=[("t_d", tdone), ("y_st", stash_st.get("k", 0))])
                    p.op("y", lambda e, c=c: e.dma_start(out=kd[c, :, :], in_=kc_sb[:, :]),
                         waits=[("s_d", ka)], inc="y_st", inck=16)
                    stash_st["k"] = p.cur("y_st")
                    pb_free = ka
                    tdone = None
                    for eh in range(2):
                        for dc in range(DC):
                            tdone = p.op("t", lambda e, eh=eh, dc=dc, XFt=XFt: e.matmul(
                                ps_a[:, eh * 512:(eh + 1) * 512], XFt[:, dc, :],
                                wv[:, dc, eh * 512:(eh + 1) * 512],
                                start=(dc == 0), stop=(dc == 7)),
                                waits=[("s_d", pa_free_q)],
                                inc="t_d" if (eh == 1 and dc == 7) else None)
                    va = p.op("s", lambda e: e.activation(out=kt_tm[:, :], in_=ps_a[:],
                                                          func=AF.Identity),
                              waits=[("t_d", tdone), ("y_st", stash_st.get("v", 0))])
                    p.op("y", lambda e, c=c: e.dma_start(out=vd[c, :, :], in_=kt_tm[:, :]),
                         waits=[("s_d", va)], inc="y_st", inck=16)
                    stash_st["v"] = p.cur("y_st")
                    pa_free_s = va
                    tdone = None
                    for dc in range(DC):
                        tdone = p.op("t", lambda e, dc=dc, XFt=XFt: e.matmul(
                            ps_b[:, 0:128], wg1[:, dc, :], XFt[:, dc, :],
                            start=(dc == 0), stop=(dc == 7)),
                            waits=[("s_d", pb_free)],
                            inc="t_d" if dc == 7 else None)
                    ga = p.op("s", lambda e: e.activation(out=g1s[:, :], in_=ps_b[:, 0:128],
                                                          func=AF.Identity),
                              waits=[("t_d", tdone)])
                    tdone = None
                    for eh in range(2):
                        tdone = p.op("t", lambda e, eh=eh: e.matmul(
                            ps_b[:, eh * 512:(eh + 1) * 512], g1s[:, :],
                            wg2[:, eh * 512:(eh + 1) * 512],
                            start=True, stop=True),
                            waits=[("s_d", ga)],
                            inc="t_d" if eh == 1 else None)
                    ge = p.op("s", lambda e: e.activation(out=L_bf[:, :], in_=ps_b[:],
                                                          func=AF.Identity),
                              waits=[("t_d", tdone), ("y_st", stash_st.get("g", 0))])
                    p.op("y", lambda e, c=c: e.dma_start(out=gld[c, :, :], in_=L_bf[:, :]),
                         waits=[("s_d", ge)], inc="y_st", inck=16)
                    stash_st["g"] = p.cur("y_st")
                    pb_free = ge
            p.barrier()
            for c in range(NT_B):
                ldv = p.op("y", lambda e, c=c: e.dma_start(out=ql1[:, :], in_=qld[c, :, :]),
                           waits=[("v_d", p.cur("v_d"))], inc="y_ld", inck=16)
                sgd = p.op("s", lambda e: e.activation(out=sg1[:, :], in_=ql1[:, :],
                                                       func=AF.Sigmoid),
                           waits=[("y_ld", p.cur("y_ld"))])
                qv = p.op("v", lambda e, c=c: e.tensor_mul(
                    out=qs.ap().rearrange("p a b -> p (a b)")[:, c * D:(c + 1) * D],
                    in0=ql1[:, :], in1=sg1[:, :]),
                    waits=[("s_d", sgd)])
                ldv = p.op("y", lambda e, c=c: e.dma_start(out=ql1[:, :], in_=gld[c, :, :]),
                           waits=[("v_d", qv)], inc="y_ld", inck=16)
                sgd = p.op("s", lambda e: e.activation(out=sg1[:, :], in_=ql1[:, :],
                                                       func=AF.Sigmoid),
                           waits=[("y_ld", p.cur("y_ld"))])
                gv1 = p.op("v", lambda e: e.tensor_mul(out=kc_sb[:, :], in0=ql1[:, :],
                                                       in1=gnB[:, :]),
                           waits=[("s_d", sgd), ("v_d", gn_v)])
                p.op("v", lambda e, c=c: e.tensor_mul(
                    out=gg.ap().rearrange("p a b -> p (a b)")[:, c * D:(c + 1) * D],
                    in0=kc_sb[:, :], in1=sg1[:, :]),
                    waits=[("v_d", gv1)])
            p.barrier()
            mz = p.op("v", lambda e: e.memset(S8.ap().rearrange("p a b -> p (a b)"), 0.0))
            mz = p.op("v", lambda e: e.memset(Zc[:, :], 0.0), waits=[("v_d", mz)])
            mz = p.op("v", lambda e: e.memset(Mold[:, :], -1e30), waits=[("v_d", mz)])
            ld_land = {}
            p.op("y", lambda e: e.dma_start(out=kci[0][:, :], in_=kd[0, :, :]),
                 inc="y_ld", inck=16)
            p.op("y", lambda e: e.dma_start(out=vsb[0][:, :], in_=vd[0, :, :]),
                 inc="y_ld", inck=16)
            ld_land[0] = p.cur("y_ld")
            st_land = {}
            for c in range(NT_B):
                bb = c % 2
                if c + 1 < NT_B:
                    p.op("y", lambda e, c=c: e.dma_start(
                        out=kci[(c + 1) % 2][:, :], in_=kd[c + 1, :, :]),
                        waits=[("s_d", ld_land.get(("kfree", (c + 1) % 2), 0)),
                               ("v_d", ld_land.get(("kfree_v", (c + 1) % 2), 0)),
                               ("t_d", ld_land.get(("vfree", (c + 1) % 2), 0))],
                        inc="y_ld", inck=16)
                    p.op("y", lambda e, c=c: e.dma_start(
                        out=vsb[(c + 1) % 2][:, :], in_=vd[c + 1, :, :]),
                        inc="y_ld", inck=16)
                    ld_land[c + 1] = p.cur("y_ld")
                cc = 0
                for h in range(H):
                    cc = p.op("v", lambda e, h=h, bb=bb: e.reduce_max(
                        Mc[:, h:h + 1], kci[bb][:, h * 128:(h + 1) * 128], axis=AX),
                        waits=[("y_ld", ld_land[c])])
                ld_land[("kfree_v", bb)] = cc
                cc = p.op("v", lambda e: e.tensor_max(out=Mnew[:, :], in0=Mold[:, :],
                                                      in1=Mc[:, :]), waits=[("v_d", cc)])
                cc = p.op("v", lambda e: e.tensor_sub(out=dM[:, :], in0=Mold[:, :],
                                                      in1=Mnew[:, :]), waits=[("v_d", cc)])
                sc = p.op("s", lambda e: e.activation(out=rr[:, :], in_=dM[:, :],
                                                      func=AF.Exp), waits=[("v_d", cc)])
                sc = p.op("s", lambda e: e.activation(out=negM[:, :], in_=Mnew[:, :],
                                                      func=AF.Identity, scale=-1.0),
                          waits=[("s_d", sc)])
                kex = sc
                for h in range(H):
                    kex = p.op("s", lambda e, h=h, bb=bb: e.activation(
                        out=kc_sb[:, h * 128:(h + 1) * 128],
                        in_=kci[bb][:, h * 128:(h + 1) * 128],
                        func=AF.Exp, bias=negM[:, h:h + 1]),
                        waits=[("s_d", kex), ("v_d", ld_land.get("kc_used", 0))])
                ld_land[("kfree", bb)] = kex
                cc2 = 0
                for h in range(H):
                    cc2 = p.op("v", lambda e, h=h: e.tensor_scalar(
                        out=S8[:, h, :], in0=S8[:, h, :], scalar1=rr[:, h:h + 1],
                        scalar2=None, op0=ALU.mult),
                        waits=[("s_d", sc), ("v_d", mz)])
                sbc = p.op("s", lambda e: e.activation(
                    out=S8b.ap().rearrange("p a b -> p (a b)"),
                    in_=S8.ap().rearrange("p a b -> p (a b)"), func=AF.Identity),
                    waits=[("v_d", cc2)])
                zb = p.op("v", lambda e: e.tensor_mul(out=Zb[:, :], in0=Zc[:, :],
                                                      in1=rr[:, :]),
                          waits=[("s_d", sc)])
                cc = zb
                for h in range(H):
                    cc = p.op("v", lambda e, h=h: e.reduce_sum(
                        cs8[:, h:h + 1], kc_sb[:, h * 128:(h + 1) * 128], axis=AX),
                        waits=[("s_d", kex), ("v_d", cc)])
                zn = p.op("v", lambda e: e.tensor_add(out=Zc[:, :], in0=Zb[:, :],
                                                      in1=cs8[:, :]), waits=[("v_d", cc)])
                zbs = p.op("s", lambda e: e.activation(out=Zbs[:, :], in_=Zb[:, :],
                                                       func=AF.Identity),
                           waits=[("v_d", zb)])
                mo = p.op("v", lambda e: e.tensor_copy(out=Mold[:, :], in_=Mnew[:, :]),
                          waits=[("s_d", sc)])
                ktv = 0
                for g4 in range(2):
                    tdone = None
                    for j in range(4):
                        h = g4 * 4 + j
                        tdone = p.op("t", lambda e, h=h, j=j, g4=g4: e.transpose(
                            [ps_t, ps_t2][g4][:, j * 128:(j + 1) * 128],
                            kc_sb[:, h * 128:(h + 1) * 128], consts.idt[:]),
                            waits=[("s_d", kex), ("v_d", ld_land.get("ofv", 0))],
                            inc="t_d" if j == 3 else None)
                    ktv = p.op("v", lambda e, g4=g4: e.tensor_copy(
                        out=kt_tm[:, g4 * 512:(g4 + 1) * 512], in_=[ps_t, ps_t2][g4][:]),
                        waits=[("t_d", tdone)])
                tdone = None
                for eh in range(2):
                    tdone = p.op("t", lambda e, eh=eh: e.matmul(
                        ps_a[:, eh * 512:(eh + 1) * 512], consts.uincl[:, :],
                        kt_tm[:, eh * 512:(eh + 1) * 512], start=True, stop=True),
                        waits=[("v_d", ktv), ("g_ld", consts.g_ready)],
                        inc="t_d" if eh == 1 else None)
                lev = p.op("s", lambda e: e.activation(out=L_bf[:, :], in_=ps_a[:],
                                                       func=AF.Identity),
                           waits=[("t_d", tdone)])
                lcv = 0
                for g4 in range(2):
                    tdone = None
                    for j in range(4):
                        h = g4 * 4 + j
                        tdone = p.op("t", lambda e, h=h, j=j, g4=g4: e.transpose(
                            [ps_t, ps_t2][g4][:, j * 128:(j + 1) * 128],
                            L_bf[:, h * 128:(h + 1) * 128], consts.idt[:]),
                            waits=[("s_d", lev)],
                            inc="t_d" if j == 3 else None)
                    lcv = p.op("v", lambda e, g4=g4: e.tensor_copy(
                        out=Lcm[:, g4 * 512:(g4 + 1) * 512], in_=[ps_t, ps_t2][g4][:]),
                        waits=[("t_d", tdone)])
                cc = 0
                for h in range(H):
                    cc = p.op("v", lambda e, h=h: e.tensor_scalar(
                        out=Zi[:, h * 128:(h + 1) * 128],
                        in0=Lcm[:, h * 128:(h + 1) * 128],
                        scalar1=Zbs[:, h:h + 1], scalar2=None, op0=ALU.add),
                        waits=[("v_d", lcv), ("s_d", zbs)])
                sc2 = p.op("s", lambda e: e.activation(out=inv[:, :], in_=Zi[:, :],
                                                       func=AF.Ln),
                           waits=[("v_d", cc)])
                sc2 = p.op("s", lambda e: e.activation(out=inv[:, :], in_=inv[:, :],
                                                       func=AF.Exp, scale=-1.0,
                                                       bias=lnsc),
                           waits=[("s_d", sc2), ("v_d", col_ready)])
                qqv = p.op("v", lambda e, c=c: e.tensor_mul(
                    out=qq[:, :],
                    in0=qs.ap().rearrange("p a b -> p (a b)")[:, c * D:(c + 1) * D],
                    in1=inv[:, :]),
                    waits=[("s_d", sc2)])
                dbg_here = dbg is not None and b == 0 and c == dbg["chunk"]
                tdone = None
                for h in range(H):
                    tdone = p.op("t", lambda e, h=h: e.matmul(
                        ps_b[:, h * 128:(h + 1) * 128],
                        kc_sb[:, h * 128:(h + 1) * 128],
                        qq[:, h * 128:(h + 1) * 128], start=True, stop=True),
                        waits=[("v_d", qqv)],
                        inc="t_d" if h == H - 1 else None)
                atv = 0
                for h in range(H):
                    atv = p.op("v", lambda e, h=h: e.tensor_tensor(
                        out=at_sb[:, h * 128:(h + 1) * 128],
                        in0=ps_b[:, h * 128:(h + 1) * 128],
                        in1=consts.uincl32[:, :], op=ALU.mult),
                        waits=[("t_d", tdone), ("y_ld", consts.y_ready)])
                tdone = None
                for h in range(H):
                    p.op("t", lambda e, h=h: e.matmul(
                        ps_o[:, h * 128:(h + 1) * 128], qq[:, h * 128:(h + 1) * 128],
                        S8b[:, h, :], start=True, stop=False),
                        waits=[("s_d", sbc), ("v_d", atv)])
                    tdone = p.op("t", lambda e, h=h, bb=bb: e.matmul(
                        ps_o[:, h * 128:(h + 1) * 128], at_sb[:, h * 128:(h + 1) * 128],
                        vsb[bb][:, h * 128:(h + 1) * 128], start=False, stop=True),
                        inc="t_d" if h == H - 1 else None)
                o_done = tdone
                if dbg_here:
                    dvv = p.op("v", lambda e: e.tensor_copy(out=dbg["o_scr"][:, :],
                                                            in_=ps_o[:]),
                               waits=[("t_d", o_done)])
                    p.op("y", lambda e: e.dma_start(out=dbg["d_o"][:, :],
                                                    in_=dbg["o_scr"][:, :]),
                         waits=[("v_d", dvv)], inc="y_st", inck=16)
                    p.barrier()
                    for nm, src_t in [("d_kc", kc_sb), ("d_ktm", kt_tm),
                                      ("d_lbf", L_bf), ("d_at", at_sb),
                                      ("d_inv", inv), ("d_qq", qq)]:
                        cpv = p.op("v", lambda e, src_t=src_t: e.tensor_copy(
                            out=dbg["o_scr"][:, :], in_=src_t[:, :]))
                        p.op("y", lambda e, nm=nm: e.dma_start(
                            out=dbg[nm][:, :], in_=dbg["o_scr"][:, :]),
                            waits=[("v_d", cpv)], inc="y_st", inck=16)
                        p.barrier()
                tdone = None
                for h in range(H):
                    tdone = p.op("t", lambda e, h=h, bb=bb: e.matmul(
                        ps_a[:, h * 128:(h + 1) * 128], kt_tm[:, h * 128:(h + 1) * 128],
                        vsb[bb][:, h * 128:(h + 1) * 128], start=True, stop=True),
                        waits=[("s_d", lev)],
                        inc="t_d" if h == H - 1 else None)
                ld_land[("vfree", bb)] = tdone
                sadd_v = p.op("v", lambda e: e.tensor_add(
                    out=S8.ap().rearrange("p a b -> p (a b)"),
                    in0=S8.ap().rearrange("p a b -> p (a b)"), in1=ps_a[:]),
                    waits=[("t_d", tdone), ("s_d", sbc)])
                ld_land["kc_used"] = sadd_v
                sq = p.op("s", lambda e: e.activation(out=osq[:, :], in_=ps_o[:],
                                                      func=AF.Square),
                          waits=[("t_d", o_done)])
                rs = p.op("v", lambda e: e.reduce_sum(ssum[:, 0:1], osq[:, :], axis=AX),
                          waits=[("s_d", sq)])
                rs2 = p.op("s", lambda e: e.activation(out=ssum[:, 1:2], in_=ssum[:, 0:1],
                                                       func=AF.Ln, scale=1.0 / D,
                                                       bias=epsc),
                           waits=[("v_d", rs)])
                rs2 = p.op("s", lambda e: e.activation(out=ssum[:, 1:2], in_=ssum[:, 1:2],
                                                       func=AF.Exp, scale=-0.5),
                           waits=[("s_d", rs2)])
                import os as _os
                if _os.environ.get("LN_SKIP_EPI"):
                    ov = p.op("v", lambda e: e.tensor_copy(out=o3[:, :], in_=ps_o[:]),
                              waits=[("s_d", rs2)])
                elif _os.environ.get("LN_ONLY_RSTD"):
                    ov = p.op("v", lambda e: e.tensor_scalar(
                        out=o3[:, :], in0=ps_o[:], scalar1=ssum[:, 1:2], scalar2=None,
                        op0=ALU.mult), waits=[("s_d", rs2)])
                elif _os.environ.get("LN_ONLY_GN"):
                    ov = p.op("v", lambda e: e.tensor_tensor(
                        out=o3[:, :], in0=ps_o[:], in1=gnB[:, :], op=ALU.mult),
                        waits=[("s_d", rs2), ("v_d", gn_v)])
                elif _os.environ.get("LN_ONLY_GG"):
                    ov = p.op("v", lambda e, c=c: e.tensor_tensor(
                        out=o3[:, :], in0=ps_o[:],
                        in1=gg.ap().rearrange("p a b -> p (a b)")[:, c * D:(c + 1) * D],
                        op=ALU.mult),
                        waits=[("s_d", rs2)])
                else:
                    ov = p.op("v", lambda e: e.tensor_scalar(
                        out=osq[:, :], in0=ps_o[:], scalar1=ssum[:, 1:2], scalar2=None,
                        op0=ALU.mult), waits=[("s_d", rs2), ("v_d", rs)])
                    ov = p.op("v", lambda e, c=c: e.tensor_mul(
                        out=o3[:, :], in0=osq[:, :],
                        in1=gg.ap().rearrange("p a b -> p (a b)")[:, c * D:(c + 1) * D]),
                        waits=[("v_d", ov)])
                ofv = 0
                for g4 in range(2):
                    tdone = None
                    for j in range(4):
                        h = g4 * 4 + j
                        tdone = p.op("t", lambda e, h=h, j=j, g4=g4: e.transpose(
                            [ps_t, ps_t2][g4][:, j * 128:(j + 1) * 128],
                            o3[:, h * 128:(h + 1) * 128], consts.idt[:]),
                            waits=[("v_d", ov)],
                            inc="t_d" if j == 3 else None)
                    ofv = p.op("v", lambda e, g4=g4: e.tensor_copy(
                        out=ofm[:, g4 * 4:(g4 + 1) * 4, :],
                        in_=[ps_t, ps_t2][g4].ap().rearrange("p (a b) -> p a b", a=4)),
                        waits=[("t_d", tdone)])
                ld_land["ofv"] = ofv
                tdone = None
                for eh in range(2):
                    for dc in range(DC):
                        tdone = p.op("t", lambda e, eh=eh, dc=dc: e.matmul(
                            ps_b[:, eh * 512:(eh + 1) * 512], ofm[:, dc, :],
                            wo[:, dc, eh * 512:(eh + 1) * 512],
                            start=(dc == 0), stop=(dc == 7)),
                            waits=[("v_d", ofv)],
                            inc="t_d" if (eh == 1 and dc == 7) else None)
                t_glob = b * NT_B + c
                p.op("y", lambda e, t_glob=t_glob, bb=bb: e.dma_start(
                    out=xr[bb][:], in_=stream_in[t_glob * 128:(t_glob + 1) * 128, :]),
                    waits=[("v_d", ld_land.get(("epv", bb), 0))],
                    inc="y_ld", inck=16)
                epv = p.op("v", lambda e, bb=bb: e.tensor_tensor(
                    out=osb[bb][:], in0=xr[bb][:], in1=ps_b[:], op=ALU.add),
                    waits=[("t_d", tdone), ("y_ld", p.cur("y_ld")),
                           ("y_st", st_land.get(c - 1, 0))])
                p.op("y", lambda e, t_glob=t_glob, bb=bb: e.dma_start(
                    out=stream_out[t_glob * 128:(t_glob + 1) * 128, :], in_=osb[bb][:]),
                    waits=[("v_d", epv)], inc="y_st", inck=16)
                st_land[c] = p.cur("y_st")
                ld_land[("epv", bb)] = epv
            p.barrier()



_BUILD_CACHE = {}


def build_test_s0():
    nc = bass.Bass("TRN2", target_bir_lowering=False, debug=False)
    dins = {}
    dins["x_in"] = nc.dram_tensor("x_in", [TOK, D], F32, kind="ExternalInput").ap()
    dins["L0_ffn_gate_kernel"] = nc.dram_tensor("L0_ffn_gate_kernel", [D, 8], F32, kind="ExternalInput").ap()
    dins["L0_ffn_w0_kernel"] = nc.dram_tensor("L0_ffn_w0_kernel", [8, D, 2048], F32, kind="ExternalInput").ap()
    dins["L0_ffn_w1_kernel"] = nc.dram_tensor("L0_ffn_w1_kernel", [8, D, 2048], F32, kind="ExternalInput").ap()
    dins["L0_ffn_wo_kernel"] = nc.dram_tensor("L0_ffn_wo_kernel", [8, 2048, D], F32, kind="ExternalInput").ap()
    for k, v in host_consts().items():
        dins[k] = nc.dram_tensor(k, list(v.shape), F32, kind="ExternalInput").ap()
    out = nc.dram_tensor("out", [TOK, D], F32, kind="ExternalOutput").ap()
    with ExitStack() as stack:
        p = P(nc, stack)
        p.nc_inputs = dins
        p.std_sems()
        consts = Consts(p, stack)
        emit_s_ffn_l0(p, dins, consts, dins["x_in"], out)
        p.finalize()
        p.build_block()
    return nc


def _weight_dins(nc, names_shapes):
    d = {}
    for nm, sh in names_shapes:
        d[nm] = nc.dram_tensor(nm, sh, F32, kind="ExternalInput").ap()
    return d


def build_launch_A():
    nc = bass.Bass("TRN2", target_bir_lowering=False, debug=False)
    dins = _weight_dins(nc, [
        ("x_in", [TOK, D]),
        ("L0_ffn_gate_kernel", [D, 8]),
        ("L0_ffn_w0_kernel", [8, D, 2048]),
        ("L0_ffn_w1_kernel", [8, D, 2048]),
        ("L0_ffn_wo_kernel", [8, 2048, D]),
        ("L0_attn_qk", [D, D]),
        ("L0_attn_ov", [D, D]),
        ("L1_ffn_w_gate", [D, 8]),
    ])
    for k, v in host_consts().items():
        dins[k] = nc.dram_tensor(k, list(v.shape), F32, kind="ExternalInput").ap()
    sA = nc.dram_tensor("sA", [TOK, D], F32, kind="Internal").ap()
    sB = nc.dram_tensor("sB", [TOK, D], F32, kind="Internal").ap()
    stream3 = nc.dram_tensor("stream3", [TOK, D], F32, kind="ExternalOutput").ap()
    t8o = nc.dram_tensor("t8o", [128, NT, 8], F32, kind="ExternalOutput").ap()
    w8o = nc.dram_tensor("w8o", [128, NT, 8], F32, kind="ExternalOutput").ap()
    cnto = nc.dram_tensor("cnto", [1, 8], F32, kind="ExternalOutput").ap()
    with ExitStack() as stack:
        p = P(nc, stack)
        p.nc_inputs = dins
        p.std_sems()
        consts = Consts(p, stack)
        emit_s_ffn_l0(p, dins, consts, dins["x_in"], sA)
        p.barrier()
        emit_attn_l0(p, dins, consts, sA, sB)
        p.barrier()
        emit_s_ffn_l0(p, dins, consts, sB, stream3)
        p.barrier()
        emit_ffn_stats(p, dins, consts, stream3, dins["L1_ffn_w_gate"],
                       t8o, w8o, cnto)
        p.finalize()
        p.build_block()
    return nc


def build_launch_B():
    nc = bass.Bass("TRN2", target_bir_lowering=False, debug=False)
    dins = _weight_dins(nc, [
        ("stream3", [TOK, D]),
        ("t8i", [128, NT, 8]),
        ("w8ri", [128, NT, 8]),
        ("counts_all", [8, 8]),
        ("core_lt", [8, 1]),
        ("L1_ffn_c_fc", [8, D, 3072]),
        ("L1_ffn_c_proj", [8, 1536, D]),
        ("L1_ffn_w_gate", [D, 8]),
        ("L1_attn_WqT", [D, D]),
        ("L1_attn_WkT", [D, D]),
        ("L1_attn_WvT", [D, D]),
        ("L1_attn_WoT", [D, D]),
        ("L1_attn_Wg1T", [D, 128]),
        ("L1_attn_Wg2T", [128, D]),
        ("L1_attn_g_norm_weight", [D]),
    ])
    for k, v in host_consts().items():
        dins[k] = nc.dram_tensor(k, list(v.shape), F32, kind="ExternalInput").ap()
    sA = nc.dram_tensor("sA", [TOK, D], F32, kind="Internal").ap()
    stream5 = nc.dram_tensor("stream5", [TOK, D], F32, kind="ExternalOutput").ap()
    t8o = nc.dram_tensor("t8o", [128, NT, 8], F32, kind="ExternalOutput").ap()
    w8o = nc.dram_tensor("w8o", [128, NT, 8], F32, kind="ExternalOutput").ap()
    cnto = nc.dram_tensor("cnto", [1, 8], F32, kind="ExternalOutput").ap()
    with ExitStack() as stack:
        p = P(nc, stack)
        p.nc_inputs = dins
        p.std_sems()
        consts = Consts(p, stack)
        emit_s_ffn_l1(p, dins, consts, dins["stream3"], sA, dins["t8i"],
                      dins["w8ri"], dins["counts_all"], dins["core_lt"])
        p.barrier()
        emit_lightnet(p, dins, consts, sA, stream5)
        p.barrier()
        emit_ffn_stats(p, dins, consts, stream5, dins["L1_ffn_w_gate"],
                       t8o, w8o, cnto)
        p.finalize()
        p.build_block()
    return nc


def build_launch_C():
    nc = bass.Bass("TRN2", target_bir_lowering=False, debug=False)
    dins = _weight_dins(nc, [
        ("stream5", [TOK, D]),
        ("t8i", [128, NT, 8]),
        ("w8ri", [128, NT, 8]),
        ("counts_all", [8, 8]),
        ("core_lt", [8, 1]),
        ("L1_ffn_c_fc", [8, D, 3072]),
        ("L1_ffn_c_proj", [8, 1536, D]),
    ])
    for k, v in host_consts().items():
        dins[k] = nc.dram_tensor(k, list(v.shape), F32, kind="ExternalInput").ap()
    out = nc.dram_tensor("out", [TOK, D], F32, kind="ExternalOutput").ap()
    with ExitStack() as stack:
        p = P(nc, stack)
        p.nc_inputs = dins
        p.std_sems()
        consts = Consts(p, stack)
        emit_s_ffn_l1(p, dins, consts, dins["stream5"], out, dins["t8i"],
                      dins["w8ri"], dins["counts_all"], dins["core_lt"])
        p.finalize()
        p.build_block()
    return nc


def kernel(**inputs):
    if "progs" not in _BUILD_CACHE:
        _BUILD_CACHE["progs"] = (build_launch_A(), build_launch_B(), build_launch_C())
    ncA, ncB, ncC = _BUILD_CACHE["progs"]
    hc = host_consts()
    x = np.ascontiguousarray(inputs["L0_attn_input_data"])
    wT = {
        "L1_attn_WqT": np.ascontiguousarray(inputs["L1_attn_Wq"].T),
        "L1_attn_WkT": np.ascontiguousarray(inputs["L1_attn_Wk"].T),
        "L1_attn_WvT": np.ascontiguousarray(inputs["L1_attn_Wv"].T),
        "L1_attn_WoT": np.ascontiguousarray(inputs["L1_attn_Wo"].T),
        "L1_attn_Wg1T": np.ascontiguousarray(inputs["L1_attn_Wg1"].T),
        "L1_attn_Wg2T": np.ascontiguousarray(inputs["L1_attn_Wg2"].T),
    }
    core_ids = list(range(NCORES))
    in_maps = []
    for c in core_ids:
        im = {"x_in": np.ascontiguousarray(x[2 * c:2 * c + 2].reshape(TOK, D)),
              "L0_ffn_gate_kernel": inputs["L0_ffn_gate_kernel"],
              "L0_ffn_w0_kernel": inputs["L0_ffn_w0_kernel"],
              "L0_ffn_w1_kernel": inputs["L0_ffn_w1_kernel"],
              "L0_ffn_wo_kernel": inputs["L0_ffn_wo_kernel"],
              "L0_attn_qk": inputs["L0_attn_qk"],
              "L0_attn_ov": inputs["L0_attn_ov"],
              "L1_ffn_w_gate": inputs["L1_ffn_w_gate"]}
        im.update(hc)
        in_maps.append(im)
    resA = run_bass_kernel_spmd(ncA, in_maps, core_ids=core_ids)
    counts3 = np.concatenate([resA.results[c]["cnto"] for c in core_ids], 0)
    in_maps = []
    for c in core_ids:
        clt = (np.arange(8) < c).astype(np.float32)[:, None]
        im = {"stream3": resA.results[c]["stream3"],
              "t8i": resA.results[c]["t8o"], "w8ri": resA.results[c]["w8o"],
              "counts_all": counts3, "core_lt": clt,
              "L1_ffn_c_fc": inputs["L1_ffn_c_fc"],
              "L1_ffn_c_proj": inputs["L1_ffn_c_proj"],
              "L1_ffn_w_gate": inputs["L1_ffn_w_gate"],
              "L1_attn_g_norm_weight": inputs["L1_attn_g_norm_weight"]}
        im.update(wT)
        im.update(hc)
        in_maps.append(im)
    resB = run_bass_kernel_spmd(ncB, in_maps, core_ids=core_ids)
    counts5 = np.concatenate([resB.results[c]["cnto"] for c in core_ids], 0)
    in_maps = []
    for c in core_ids:
        clt = (np.arange(8) < c).astype(np.float32)[:, None]
        im = {"stream5": resB.results[c]["stream5"],
              "t8i": resB.results[c]["t8o"], "w8ri": resB.results[c]["w8o"],
              "counts_all": counts5, "core_lt": clt,
              "L1_ffn_c_fc": inputs["L1_ffn_c_fc"],
              "L1_ffn_c_proj": inputs["L1_ffn_c_proj"]}
        im.update(hc)
        in_maps.append(im)
    resC = run_bass_kernel_spmd(ncC, in_maps, core_ids=core_ids)
    out = np.concatenate([resC.results[c]["out"] for c in core_ids], 0)
    return out.reshape(16, SEQ, D)


def build_test_s1():
    nc = bass.Bass("TRN2", target_bir_lowering=False, debug=False)
    dins = {}
    dins["x_in"] = nc.dram_tensor("x_in", [TOK, D], F32, kind="ExternalInput").ap()
    dins["L0_attn_qk"] = nc.dram_tensor("L0_attn_qk", [D, D], F32, kind="ExternalInput").ap()
    dins["L0_attn_ov"] = nc.dram_tensor("L0_attn_ov", [D, D], F32, kind="ExternalInput").ap()
    for k, v in host_consts().items():
        dins[k] = nc.dram_tensor(k, list(v.shape), F32, kind="ExternalInput").ap()
    out = nc.dram_tensor("out", [TOK, D], F32, kind="ExternalOutput").ap()
    with ExitStack() as stack:
        p = P(nc, stack)
        p.nc_inputs = dins
        p.std_sems()
        consts = Consts(p, stack)
        emit_attn_l0(p, dins, consts, dins["x_in"], out)
        p.finalize()
        p.build_block()
    return nc


def build_test_s3stats():
    nc = bass.Bass("TRN2", target_bir_lowering=False, debug=False)
    dins = {}
    dins["x_in"] = nc.dram_tensor("x_in", [TOK, D], F32, kind="ExternalInput").ap()
    dins["w_gate"] = nc.dram_tensor("w_gate", [D, 8], F32, kind="ExternalInput").ap()
    for k, v in host_consts().items():
        dins[k] = nc.dram_tensor(k, list(v.shape), F32, kind="ExternalInput").ap()
    t8o = nc.dram_tensor("t8o", [128, NT, 8], F32, kind="ExternalOutput").ap()
    w8o = nc.dram_tensor("w8o", [128, NT, 8], F32, kind="ExternalOutput").ap()
    cnto = nc.dram_tensor("cnto", [1, 8], F32, kind="ExternalOutput").ap()
    with ExitStack() as stack:
        p = P(nc, stack)
        p.nc_inputs = dins
        p.std_sems()
        consts = Consts(p, stack)
        emit_ffn_stats(p, dins, consts, dins["x_in"], dins["w_gate"], t8o, w8o, cnto)
        p.finalize()
        p.build_block()
    return nc


def build_test_s3():
    nc = bass.Bass("TRN2", target_bir_lowering=False, debug=False)
    dins = {}
    dins["x_in"] = nc.dram_tensor("x_in", [TOK, D], F32, kind="ExternalInput").ap()
    dins["t8i"] = nc.dram_tensor("t8i", [128, NT, 8], F32, kind="ExternalInput").ap()
    dins["w8ri"] = nc.dram_tensor("w8ri", [128, NT, 8], F32, kind="ExternalInput").ap()
    dins["counts_all"] = nc.dram_tensor("counts_all", [8, 8], F32, kind="ExternalInput").ap()
    dins["core_lt"] = nc.dram_tensor("core_lt", [8, 1], F32, kind="ExternalInput").ap()
    dins["L1_ffn_c_fc"] = nc.dram_tensor("L1_ffn_c_fc", [8, D, 3072], F32, kind="ExternalInput").ap()
    dins["L1_ffn_c_proj"] = nc.dram_tensor("L1_ffn_c_proj", [8, 1536, D], F32, kind="ExternalInput").ap()
    for k, v in host_consts().items():
        dins[k] = nc.dram_tensor(k, list(v.shape), F32, kind="ExternalInput").ap()
    out = nc.dram_tensor("out", [TOK, D], F32, kind="ExternalOutput").ap()
    with ExitStack() as stack:
        p = P(nc, stack)
        p.nc_inputs = dins
        p.std_sems()
        consts = Consts(p, stack)
        emit_s_ffn_l1(p, dins, consts, dins["x_in"], out, dins["t8i"],
                      dins["w8ri"], dins["counts_all"], dins["core_lt"])
        p.finalize()
        p.build_block()
    return nc


def build_test_s4():
    nc = bass.Bass("TRN2", target_bir_lowering=False, debug=False)
    dins = {}
    dins["x_in"] = nc.dram_tensor("x_in", [TOK, D], F32, kind="ExternalInput").ap()
    for nm, sh in [("L1_attn_WqT", [D, D]), ("L1_attn_WkT", [D, D]),
                   ("L1_attn_WvT", [D, D]), ("L1_attn_WoT", [D, D]),
                   ("L1_attn_Wg1T", [D, 128]), ("L1_attn_Wg2T", [128, D]),
                   ("L1_attn_g_norm_weight", [D])]:
        dins[nm] = nc.dram_tensor(nm, sh, F32, kind="ExternalInput").ap()
    for k, v in host_consts().items():
        dins[k] = nc.dram_tensor(k, list(v.shape), F32, kind="ExternalInput").ap()
    out = nc.dram_tensor("out", [TOK, D], F32, kind="ExternalOutput").ap()
    dbg = {"chunk": 0}
    for nm in ["d_kc", "d_ktm", "d_lbf", "d_lcm", "d_inv", "d_qq", "d_o", "d_at"]:
        dbg[nm] = nc.dram_tensor(nm, [128, D], F32, kind="ExternalOutput").ap()
    with ExitStack() as stack:
        p = P(nc, stack)
        p.nc_inputs = dins
        p.std_sems()
        consts = Consts(p, stack)
        emit_lightnet(p, dins, consts, dins["x_in"], out, dbg=dbg)
        p.finalize()
        p.build_block()
    return nc


def build_test_gate(n_tiles=2):
    nc = bass.Bass("TRN2", target_bir_lowering=False, debug=False)
    dins = {}
    dins["x_in"] = nc.dram_tensor("x_in", [n_tiles * 128, D], F32, kind="ExternalInput").ap()
    dins["L0_ffn_gate_kernel"] = nc.dram_tensor("L0_ffn_gate_kernel", [D, 8], F32, kind="ExternalInput").ap()
    for k, v in host_consts().items():
        dins[k] = nc.dram_tensor(k, list(v.shape), F32, kind="ExternalInput").ap()
    out_w8 = nc.dram_tensor("out_w8", [128, NT, 8], F32, kind="ExternalOutput").ap()
    out_gl = nc.dram_tensor("out_gl", [128, NT, 8], F32, kind="ExternalOutput").ap()
    out_xf = nc.dram_tensor("out_xf", [128, DC, n_tiles * 128], F32, kind="ExternalOutput").ap()
    with ExitStack() as stack:
        p = P(nc, stack)
        p.nc_inputs = dins
        p.std_sems()
        consts = Consts(p, stack)
        XF = stack.enter_context(nc.sbuf_tensor(p.name("ff_XF"), [128, DC, n_tiles * 128], BF16))
        W8 = stack.enter_context(nc.sbuf_tensor(p.name("ff_W8"), [128, NT, 8], F32))
        gk = stack.enter_context(nc.sbuf_tensor(p.name("ff_gk"), [128, DC, 8], BF16))
        logits = stack.enter_context(nc.sbuf_tensor(p.name("ff_gl"), [128, NT, 8], F32))
        XF32 = stack.enter_context(nc.sbuf_tensor(p.name("ff_XF32"), [128, DC, n_tiles * 128], F32))
        p.op("g", lambda e: e.dma_start(
            out=gk[:, :, :],
            in_=dins["L0_ffn_gate_kernel"][:, :].rearrange("(dc p) e -> p dc e", p=128)),
            inc="g_ld", inck=16)
        gk_ready = p.cur("g_ld")
        with ExitStack() as stp:
            emit_prep(p, stp, dins["x_in"], None, XF, consts,
                      gate={"gk_sb": gk, "logits_sb": logits, "ready": gk_ready}, n_tiles=n_tiles)
            p.barrier()
        with ExitStack() as stg:
            emit_gate_l0(p, stg, logits, W8, n_tiles=n_tiles)
            p.barrier()
        p.op("v", lambda e: e.tensor_copy(out=XF32[:, :, :], in_=XF[:, :, :]), inc="v_d")
        p.op("y", lambda e: e.dma_start(out=out_w8[:, :, :], in_=W8[:, :, :]),
             waits=[("v_d", p.cur("v_d"))], inc="y_st", inck=16)
        p.op("y", lambda e: e.dma_start(out=out_gl[:, :, :], in_=logits[:, :, :]), inc="y_st", inck=16)
        p.op("y", lambda e: e.dma_start(out=out_xf[:, :, :], in_=XF32[:, :, :]), inc="y_st", inck=16)
        p.finalize()
        p.build_block()
    return nc
